# version 04 (final)
# baseline (speedup 1.0000x reference)
import contextlib
import numpy as np
import ml_dtypes
import concourse.bass as bass
import concourse.mybir as mybir
from concourse.bass_utils import run_bass_kernel_spmd

F32 = mybir.dt.float32
BF16 = mybir.dt.bfloat16
AF = mybir.ActivationFunctionType
ALU = mybir.AluOpType
AX = mybir.AxisListType

D = 1024
BATCH = 4
NCORE = 8
S = 4096
BLK = 256
TOK = 2048
DT = D // 128
NH = 16
DK = 64
EPS = 1e-5


class P:

    def __init__(self, nc):
        self.nc = nc
        self._sems = {}

    def _s(self, name):
        if name not in self._sems:
            self._sems[name] = [self.nc.alloc_semaphore(name), 0]
        return self._sems[name]

    def inc(self, name, inst, k=1):
        s = self._s(name)
        inst.then_inc(s[0], k)
        s[1] += k
        return s[1]

    def inc16(self, name, inst):
        return self.inc(name, inst, 16)

    def n(self, name):
        return self._s(name)[1]

    def wait(self, eng, name, val=None):
        s = self._s(name)
        w = s[1] if val is None else val
        if w > 0:
            eng.wait_ge(s[0], w)

    def join(self, engines, sems=None):
        if sems is None:
            sems = list(self._sems)
        for e in engines:
            for sn in sems:
                self.wait(e, sn)


def chunk_map(rt, base):
    c = 128 * rt + base
    return c // 192, (c % 192) // 64


def build_launch1(upto="wo", dbg=()):
    nc = bass.Bass("TRN2", target_bir_lowering=False, num_devices=8)
    p = P(nc)
    v, sc, g, t, sy = nc.vector, nc.scalar, nc.gpsimd, nc.tensor, nc.sync

    x_fullT = nc.dram_tensor("x_fullT", [D, S], F32, kind="ExternalInput").ap()
    x_ownT = nc.dram_tensor("x_ownT", [D, TOK], F32, kind="ExternalInput").ap()
    w_qkv = nc.dram_tensor("w_qkv", [3 * D, D], F32, kind="ExternalInput").ap()
    b_qkv = nc.dram_tensor("b_qkv", [1, 3 * D], F32, kind="ExternalInput").ap()
    w_o0 = nc.dram_tensor("w_o0", [D, D], F32, kind="ExternalInput").ap()
    b_o0 = nc.dram_tensor("b_o0", [1, D], F32, kind="ExternalInput").ap()

    w1_d = nc.dram_tensor("w1_d", [8, D, 4 * D], F32, kind="ExternalInput").ap()
    b1_d = nc.dram_tensor("b1_d", [8, 4 * D], F32, kind="ExternalInput").ap()
    w2_d = nc.dram_tensor("w2_d", [8, 4 * D, D], F32, kind="ExternalInput").ap()
    b2_d = nc.dram_tensor("b2_d", [8, D], F32, kind="ExternalInput").ap()
    wr_d = nc.dram_tensor("wr_d", [D, 8], F32, kind="ExternalInput").ap()
    br_d = nc.dram_tensor("br_d", [1, 8], F32, kind="ExternalInput").ap()
    wdq_d = nc.dram_tensor("wdq_d", [D, 256], F32, kind="ExternalInput").ap()
    wdkv_d = nc.dram_tensor("wdkv_d", [D, 256], F32, kind="ExternalInput").ap()

    attn_scr = nc.dram_tensor("attn_scr", [D, TOK], F32, kind="Internal").ap()

    xo_T = nc.dram_tensor("xo_T", [D, TOK], F32, kind="ExternalOutput").ap()
    cq_o = nc.dram_tensor("cq_o", [256, TOK], BF16, kind="ExternalOutput").ap()
    ckv_o = nc.dram_tensor("ckv_o", [256, TOK], BF16, kind="ExternalOutput").ap()
    dbg_t = {}
    if "qkv" in dbg:
        dbg_t["q"] = nc.dram_tensor("dbg_q", [128, TOK], F32, kind="ExternalOutput").ap()
        dbg_t["k"] = nc.dram_tensor("dbg_k", [128, S], F32, kind="ExternalOutput").ap()
        dbg_t["hv"] = nc.dram_tensor("dbg_hv", [512, 16 * 65], F32, kind="ExternalOutput").ap()
        dbg_t["_use_g"] = True
    if "ofull" in dbg:
        dbg_t["ofull"] = nc.dram_tensor("dbg_ofull", [D, TOK], F32, kind="ExternalOutput").ap()
    if "wt" in dbg:
        dbg_t["wt"] = nc.dram_tensor("dbg_wt", [128, 128], F32, kind="ExternalOutput").ap()
        dbg_t["x1"] = nc.dram_tensor("dbg_x1", [D, TOK], F32, kind="ExternalOutput").ap()
        dbg_t["xln1"] = nc.dram_tensor("dbg_xln1", [D, TOK], F32, kind="ExternalOutput").ap()

    ident_bf = nc.alloc_sbuf_tensor("ident_bf", [128, 128], BF16)
    ident_f = nc.alloc_sbuf_tensor("ident_f", [128, 128], F32)
    ones1_f = nc.alloc_sbuf_tensor("ones1_f", [1, 128], F32)
    onescol = nc.alloc_sbuf_tensor("onescol", [128, 1], F32)
    halfsel = nc.alloc_sbuf_tensor("halfsel", [128, 2], F32)
    stat2 = nc.alloc_sbuf_tensor("stat2", [128, 4], F32)
    eps_sb = nc.alloc_sbuf_tensor("eps_sb", [1, 1], F32)
    b_qkvT = nc.alloc_sbuf_tensor("b_qkvT", [128, 24], F32)
    b_o0T = nc.alloc_sbuf_tensor("b_o0T", [128, 8], F32)

    v.memset(ident_bf[:], 1.0)
    v.memset(ident_f[:], 1.0)
    v.memset(ones1_f[:], 1.0)
    v.memset(onescol[:], 1.0)
    v.memset(eps_sb[:], EPS)
    p.inc("dve", v.memset(halfsel[:], 0.0))
    p.inc("dve", v.memset(stat2[:], 0.0))
    p.wait(v, "dve")
    v.memset(halfsel[0:64, 0:1], 1.0)
    v.memset(halfsel[64:128, 1:2], 1.0)
    v.memset(stat2[:, 0:1], 1.0)
    p.inc("dve", v.memset(stat2[:, 3:4], 1.0))
    p.wait(g, "dve")
    p.inc("pool", g.affine_select(ident_bf[:], ident_bf[:], [[-1, 128]],
                                  ALU.is_equal, 0.0, base=0, channel_multiplier=1))
    p.inc("pool", g.affine_select(ident_f[:], ident_f[:], [[-1, 128]],
                                  ALU.is_equal, 0.0, base=0, channel_multiplier=1))
    bq2 = b_qkv.rearrange("o (c p) -> (o c) p", p=128)
    bo2 = b_o0.rearrange("o (c p) -> (o c) p", p=128)
    for c in range(24):
        p.inc16("hw", sy.dma_start(out=b_qkvT[:, c:c + 1],
                                 in_=bq2[c:c + 1, :].rearrange("a b -> b a")))
    for c in range(8):
        p.inc16("hw", sy.dma_start(out=b_o0T[:, c:c + 1],
                                 in_=bo2[c:c + 1, :].rearrange("a b -> b a")))

    ps = [nc.alloc_psum_tensor(f"ps{i}", [128, 512], F32) for i in range(7)]
    ps_tr = nc.alloc_psum_tensor("ps_tr", [128, 128], BF16)

    ostack = contextlib.ExitStack()
    o_fullT = ostack.enter_context(nc.sbuf_tensor("o_fullT", [128, DT * TOK], BF16))
    with contextlib.ExitStack() as stack:
        al = lambda n, sh, dt_: stack.enter_context(nc.sbuf_tensor(n, sh, dt_))
        bq_row = al("bq_row", [1, 3 * D], F32)
        p.inc16("hw", sy.dma_start(out=bq_row[:], in_=b_qkv))
        wqkvT = al("wqkvT", [128, DT * 3 * D], BF16)
        wq_raw = al("wq_raw", [128, 2 * D], BF16)
        xf_ch = al("xf_ch", [128, DT * 512], F32)
        xln_ch = al("xln_ch", [128, DT * 512], BF16)
        sq_ch = al("sq_ch", [128, 512], F32)
        sq_big = al("sq_big", [128, S], F32)
        ab_bc = al("ab_bc", [128, 1024], F32)
        mrow = al("mrow", [1, 512], F32)
        erow = al("erow", [1, 512], F32)
        arow = al("arow", [1, 512], F32)
        brow = al("brow", [1, 512], F32)
        srow = al("srow", [1, 512], F32)
        h_tmp = al("h_tmp", [128, 1024], BF16)
        q_ab = al("q_ab", [128, TOK], BF16)
        k2 = al("k2", [128, S], BF16)
        h_v = al("h_v", [128, 4 * 16 * 65], BF16)
        pt = al("pt", [128, 2 * 1024], BF16)
        nrm = al("nrm", [1, 8], F32)
        bvals = al("bvals", [1, 2], F32)
        negb = al("negb", [128, 2], F32)
        rrow = al("rrow", [1, 1024], F32)
        rb_sb = al("rb_sb", [128, 1024], F32)

        p.wait(v, "pool")
        dve_tr = []
        for rt in range(24):
            buf = wq_raw[:, (rt % 2) * D:(rt % 2) * D + D]
            gsem = "gdw%d" % (rt % 2)
            if rt >= 2:
                p.wait(g, "pe")
            p.inc16(gsem, g.dma_start(out=buf, in_=w_qkv[128 * rt:128 * rt + 128, :]))
            p.wait(t, gsem)
            p.wait(t, "pool")
            for dt in range(DT):
                if rt or dt:
                    p.wait(t, "dve")
                p.inc("pe", t.transpose(ps_tr[:], buf[:, 128 * dt:128 * dt + 128], ident_bf[:]))
                p.wait(v, "pe")
                p.inc("dve", v.tensor_copy(
                    wqkvT[:, dt * 3 * D + 128 * rt: dt * 3 * D + 128 * rt + 128], ps_tr[:]))

        p.inc("dve", v.memset(h_v[:], 1.0))

        for pp in range(NH // 2):
            p.join([sy])
            for dt in range(DT):
                p.inc16("hw", sy.dma_start(out=xf_ch[:, dt * 512:dt * 512 + 512],
                                         in_=x_fullT[128 * dt:128 * dt + 128,
                                                     512 * pp:512 * pp + 512]))
            p.wait(sc, "hw")
            p.wait(t, "hw")
            for dt in range(DT):
                if dt:
                    p.wait(sc, "pe", pe_sq)
                p.inc("act", sc.square(sq_ch[:], xf_ch[:, dt * 512:dt * 512 + 512]))
                p.wait(t, "act")
                p.inc("pe", t.matmul(ps[5][0:1, :], onescol[:], xf_ch[:, dt * 512:dt * 512 + 512],
                                     start=(dt == 0), stop=(dt == DT - 1)))
                pe_sq = p.inc("pe", t.matmul(ps[6][0:1, :], onescol[:], sq_ch[:],
                                             start=(dt == 0), stop=(dt == DT - 1)))
            p.wait(v, "pe")
            p.inc("dve", v.tensor_scalar_mul(mrow[:], ps[5][0:1, :], 1.0 / D))
            p.inc("dve", v.tensor_scalar_mul(erow[:], ps[6][0:1, :], 1.0 / D))
            p.wait(v, "dve")
            p.inc("dve", v.tensor_mul(srow[:], mrow[:], mrow[:]))
            p.wait(v, "dve")
            p.inc("dve", v.tensor_sub(srow[:], erow[:], srow[:]))
            p.wait(sc, "dve")
            p.inc("act", sc.activation(srow[:], srow[:], AF.Sqrt, bias=eps_sb[:]))
            p.wait(v, "act")
            p.inc("dve", v.reciprocal(arow[:], srow[:]))
            p.wait(v, "dve")
            p.inc("dve", v.scalar_tensor_tensor(brow[:], mrow[:], -1.0,
                                                arow[:], ALU.mult, ALU.mult))
            p.wait(t, "dve")
            p.inc("pe", t.matmul(ps[4][:], ones1_f[:], arow[:], start=True, stop=True))
            p.inc("pe", t.matmul(ps[5][:], ones1_f[:], brow[:], start=True, stop=True))
            p.wait(sc, "pe")
            p.inc("act", sc.copy(ab_bc[:, 0:512], ps[4][:]))
            p.inc("act", sc.copy(ab_bc[:, 512:1024], ps[5][:]))
            p.wait(g, "act")
            p.wait(v, "act")
            for dt in range(DT):
                p.inc("pool", g.tensor_mul(xf_ch[:, dt * 512:dt * 512 + 512],
                                           xf_ch[:, dt * 512:dt * 512 + 512], ab_bc[:, 0:512]))
                p.wait(v, "pool")
                p.inc("dve", v.tensor_add(xln_ch[:, dt * 512:dt * 512 + 512],
                                          xf_ch[:, dt * 512:dt * 512 + 512], ab_bc[:, 512:1024]))

            p.wait(t, "dve")
            for rt in range(24):
                hbuf = h_tmp[:, (rt % 2) * 512:(rt % 2) * 512 + 512]
                psh = ps[rt % 2]
                if rt >= 2:
                    p.wait(t, "dve")
                for dt in range(DT):
                    p.inc("pe", t.matmul(psh[:],
                                         wqkvT[:, dt * 3 * D + 128 * rt: dt * 3 * D + 128 * rt + 128],
                                         xln_ch[:, dt * 512:dt * 512 + 512],
                                         start=(dt == 0), stop=(dt == DT - 1)))
                p.wait(sc, "pe")
                p.inc("act", sc.activation(hbuf, psh[:], AF.Identity, bias=b_qkvT[:, rt:rt + 1]))
                p.wait(v, "act")
                for base in (0, 64):
                    j, typ = chunk_map(rt, base)
                    src = hbuf[base:base + 64, :]
                    if typ == 0:
                        p.inc("dve", v.tensor_copy(q_ab[0:64, 128 * j:128 * j + 128], src[:, 0:128]))
                        p.inc("dve", v.tensor_copy(q_ab[64:128, 128 * j:128 * j + 128], src[:, 256:384]))
                    elif typ == 1:
                        p.inc("dve", v.tensor_copy(k2[0:64, 256 * j:256 * j + 256], src[:, 0:256]))
                        p.inc("dve", v.tensor_copy(k2[64:128, 256 * j:256 * j + 256], src[:, 256:512]))

            for tt in range(4):
                for half in range(2):
                    phv = ps[2 + (2 * tt + half) % 2]
                    if tt or half:
                        p.wait(t, "act", act_hv)
                    for dt in range(DT):
                        rhs = wqkvT[:, dt * 3 * D:(dt + 1) * 3 * D]
                        rhs = rhs.rearrange("p (j c) -> p j c", j=16)[:, half * 8:half * 8 + 8, 128:192]
                        p.inc("pe", t.matmul(phv[:],
                                             xln_ch[:, dt * 512 + tt * 128: dt * 512 + tt * 128 + 128],
                                             rhs, start=(dt == 0), stop=False))
                    bvrhs = bq_row.rearrange("o (j c) -> o j c", j=16)[:, half * 8:half * 8 + 8, 128:192]
                    p.inc("pe", t.matmul(phv[:], ones1_f[:], bvrhs, start=False, stop=True))
                    p.wait(sc, "pe")
                    dst = h_v[:, tt * 1040:(tt + 1) * 1040]
                    dst = dst.rearrange("p (j c) -> p j c", j=16)[:, half * 8:half * 8 + 8, 0:64]
                    act_hv = p.inc("act", sc.activation(dst, phv[:], AF.Identity))

            if "q" in dbg_t and pp == 0:
                p.join([g])
                p.inc16("gd", g.dma_start(out=dbg_t["q"], in_=q_ab[:]))
                p.inc16("gd", g.dma_start(out=dbg_t["k"], in_=k2[:]))
                for tt in range(4):
                    p.inc16("gd", g.dma_start(out=dbg_t["hv"][tt * 128:(tt + 1) * 128, :],
                                             in_=h_v[:, tt * 1040:(tt + 1) * 1040]))

            p.join([sc, t, v])
            p.inc("dve", v.memset(nrm[:], 0.0))
            for (mat, ncols, col0) in ((q_ab, TOK, 0), (k2, S, 2)):
                p.wait(sc, "pe")
                p.inc("act", sc.square(sq_big[:, 0:ncols], mat[:, 0:ncols]))
                p.wait(t, "act")
                p.wait(v, "dve")
                for ch in range(ncols // 512):
                    for hh in range(2):
                        p.inc("pe", t.matmul(ps[6][0:1, :], halfsel[:, hh:hh + 1],
                                             sq_big[:, ch * 512:ch * 512 + 512],
                                             start=True, stop=True))
                        p.wait(v, "pe")
                        p.inc("dve", v.reduce_max(srow[0:1, 0:1], ps[6][0:1, :], axis=AX.X))
                        p.wait(v, "dve")
                        p.inc("dve", v.tensor_max(nrm[0:1, col0 + hh:col0 + hh + 1],
                                                  nrm[0:1, col0 + hh:col0 + hh + 1],
                                                  srow[0:1, 0:1]))
                        p.wait(t, "dve")
            p.wait(sc, "dve")
            p.inc("act", sc.activation(nrm[0:1, 4:8], nrm[0:1, 0:4], AF.Sqrt))
            p.wait(v, "act")
            p.inc("dve", v.scalar_tensor_tensor(bvals[0:1, 0:2], nrm[0:1, 4:6], -0.125,
                                                nrm[0:1, 6:8], ALU.mult, ALU.mult))
            p.wait(t, "dve")
            p.inc("pe", t.matmul(ps[6][:, 0:2], ones1_f[:], bvals[0:1, 0:2], start=True, stop=True))
            p.wait(sc, "pe")
            p.inc("act", sc.copy(negb[:], ps[6][:, 0:2]))

            p.join([t, sc, v])
            for qb in range(4):
                exp_done = [None, None]
                pv_done = [None, None]
                for kt in range(32):
                    r = kt % 2
                    psA, psB = ps[0 + r], ps[2 + r]
                    ptbuf = pt[:, r * 1024:(r + 1) * 1024]
                    if exp_done[r] is not None:
                        p.wait(t, "act", exp_done[r])
                    p.inc("pe", t.matmul(psA[:], k2[0:64, 128 * kt:128 * kt + 128],
                                         q_ab[0:64, qb * 512:qb * 512 + 512],
                                         start=True, stop=True, tile_position=(0, 0)))
                    pe_s = p.inc("pe", t.matmul(psB[:], k2[64:128, 128 * kt:128 * kt + 128],
                                                q_ab[64:128, qb * 512:qb * 512 + 512],
                                                start=True, stop=True, tile_position=(64, 0)))
                    p.wait(sc, "pe", pe_s)
                    if pv_done[r] is not None:
                        p.wait(sc, "pe", pv_done[r])
                    p.inc("act", sc.activation(ptbuf[:, 0:512], psA[:], AF.Exp,
                                               bias=negb[:, 0:1], scale=0.125))
                    exp_done[r] = p.inc("act", sc.activation(ptbuf[:, 512:1024], psB[:], AF.Exp,
                                                             bias=negb[:, 1:2], scale=0.125))
                    p.wait(t, "act", exp_done[r])
                    vA = h_v[:, (kt % 2) * 1040 + 65 * (kt // 2): (kt % 2) * 1040 + 65 * (kt // 2) + 65]
                    vB = h_v[:, (2 + kt % 2) * 1040 + 65 * (kt // 2): (2 + kt % 2) * 1040 + 65 * (kt // 2) + 65]
                    p.inc("pe", t.matmul(ps[4][0:65, :], vA, ptbuf[:, 0:512],
                                         start=(kt == 0), stop=(kt == 31)))
                    pv_done[r] = p.inc("pe", t.matmul(ps[5][0:65, :], vB, ptbuf[:, 512:1024],
                                                      start=(kt == 0), stop=(kt == 31)))
                p.wait(v, "pe")
                p.inc("dve", v.reciprocal(rrow[0:1, 0:512], ps[4][64:65, :]))
                p.inc("dve", v.reciprocal(rrow[0:1, 512:1024], ps[5][64:65, :]))
                p.wait(t, "dve")
                p.inc("pe", t.matmul(ps[6][:], ones1_f[:], rrow[0:1, 0:512], start=True, stop=True))
                p.wait(sc, "pe")
                p.inc("act", sc.copy(rb_sb[:, 0:512], ps[6][:]))
                p.wait(t, "act")
                p.inc("pe", t.matmul(ps[6][:], ones1_f[:], rrow[0:1, 512:1024], start=True, stop=True))
                p.wait(sc, "pe")
                p.inc("act", sc.copy(rb_sb[:, 512:1024], ps[6][:]))
                p.wait(v, "act")
                for hh in range(2):
                    pso = (ps[4], ps[5])[hh]
                    src = pso[0:64, :].rearrange("p (j r vi) -> p j r vi", j=4, r=8)
                    rbs = rb_sb[0:64, hh * 512:(hh + 1) * 512].rearrange(
                        "p (j r vi) -> p j r vi", j=4, r=8)
                    dstbase = o_fullT[64 * hh:64 * hh + 64,
                                      pp * TOK + 4 * qb: pp * TOK + 4 * qb + (TOK - 4 * qb)]
                    dst = o_fullT[64 * hh:64 * hh + 64, :].rearrange(
                        "p (ppc r vi j) -> p ppc r vi j", ppc=DT, r=8, vi=16)[
                            :, pp, :, :, 4 * qb:4 * qb + 4]
                    dst = dst.rearrange("p r vi j -> p j r vi")
                    p.inc("dve", v.tensor_tensor(dst, src, rbs, ALU.mult))
                p.wait(t, "dve")

        if "ofull" in dbg_t:
            p.join([g])
            for dt in range(DT):
                p.inc16("gd", g.dma_start(out=dbg_t["ofull"][128 * dt:128 * dt + 128, :],
                                         in_=o_fullT[:, dt * TOK:(dt + 1) * TOK]))

    with contextlib.ExitStack() as stack:
        al = lambda n, sh, dt_: stack.enter_context(nc.sbuf_tensor(n, sh, dt_))
        woT = al("woT", [128, DT * D], BF16)
        wo_raw = al("wo_raw", [128, 2 * D], BF16)
        ao_stage = al("ao_stage", [128, 2 * 512], F32)
        p.join([g, t, v, sy])
        for ct in range(DT):
            buf = wo_raw[:, (ct % 2) * D:(ct % 2) * D + D]
            gsem = "gdw%d" % (ct % 2)
            if ct >= 2:
                p.wait(g, "pe")
            p.inc16(gsem, g.dma_start(out=buf, in_=w_o0[128 * ct:128 * ct + 128, :]))
            p.wait(t, gsem)
            for dt in range(DT):
                if ct or dt:
                    p.wait(t, "dve")
                p.inc("pe", t.transpose(ps_tr[:], buf[:, 128 * dt:128 * dt + 128], ident_bf[:]))
                p.wait(v, "pe")
                p.inc("dve", v.tensor_copy(woT[:, ct * D + 128 * dt: ct * D + 128 * dt + 128],
                                           ps_tr[:]))
        p.join([t, v])
        dve_wo = [0, 0]
        hw_wo = [0, 0]
        for i, (dt, qb) in enumerate([(a, b_) for a in range(DT) for b_ in range(4)]):
            r = i % 2
            pw = ps[r]
            stg = ao_stage[:, r * 512:(r + 1) * 512]
            if i >= 2:
                p.wait(t, "dve", dve_wo[r])
            for ct in range(DT):
                p.inc("pe", t.matmul(pw[:], woT[:, dt * D + 128 * ct: dt * D + 128 * ct + 128],
                                     o_fullT[:, ct * TOK + qb * 512: ct * TOK + qb * 512 + 512],
                                     start=(ct == 0), stop=(ct == DT - 1)))
            p.wait(v, "pe")
            if i >= 2:
                p.wait(v, "hw", hw_wo[r])
            dve_wo[r] = p.inc("dve", v.tensor_scalar_add(stg, pw[:], b_o0T[:, dt:dt + 1]))
            p.wait(sy, "dve", dve_wo[r])
            hw_wo[r] = p.inc16("hw", sy.dma_start(
                out=attn_scr[128 * dt:128 * dt + 128, qb * 512:qb * 512 + 512], in_=stg))
            p.wait(sy, "hw")

    ostack.close()

    with contextlib.ExitStack() as stack:
        al = lambda n, sh, dt_: stack.enter_context(nc.sbuf_tensor(n, sh, dt_))
        xT = al("xT", [128, DT * TOK], F32)
        xlnT = al("xlnT", [128, DT * TOK], BF16)
        ld_a = al("ld_a", [128, TOK], F32)
        p.join([sy, v])
        for dt in range(DT):
            p.inc16("hw", sy.dma_start(out=xT[:, dt * TOK:(dt + 1) * TOK],
                                       in_=x_ownT[128 * dt:128 * dt + 128, :]))
            p.inc16("hw", sy.dma_start(out=ld_a[:], in_=attn_scr[128 * dt:128 * dt + 128, :]))
            p.wait(sy, "hw")
            p.wait(v, "hw")
            p.inc("dve", v.tensor_add(xT[:, dt * TOK:(dt + 1) * TOK],
                                      xT[:, dt * TOK:(dt + 1) * TOK], ld_a[:]))
            p.wait(sy, "dve")

        C = dict(nc=nc, p=p, v=v, sc=sc, g=g, t=t, sy=sy, ps=ps, ps_tr=ps_tr,
                 ones1_f=ones1_f, onescol=onescol, eps_sb=eps_sb, ident_bf=ident_bf,
                 ident_f=ident_f, al=al)
        if "x1" in dbg_t:
            p.join([sy])
            for dt in range(DT):
                p.inc16("hw", sy.dma_start(out=dbg_t["x1"][128 * dt:128 * dt + 128, :],
                                           in_=xT[:, dt * TOK:(dt + 1) * TOK]))
            p.wait(sy, "hw")
        emit_ln(C, xT, xlnT)
        if "xln1" in dbg_t:
            p.join([g])
            for dt in range(DT):
                p.inc16("gd", g.dma_start(out=dbg_t["xln1"][128 * dt:128 * dt + 128, :],
                                          in_=xlnT[:, dt * TOK:(dt + 1) * TOK]))
        emit_ffn_l0(C, xT, xlnT, w1_d, b1_d, w2_d, b2_d, wr_d, br_d, dbg_t)
        emit_ln(C, xT, xlnT)
        emit_cqkv(C, xlnT, wdq_d, wdkv_d, cq_o, ckv_o)

        p.join([sy])
        for dt in range(DT):
            p.inc16("hw", sy.dma_start(out=xo_T[128 * dt:128 * dt + 128, :],
                                     in_=xT[:, dt * TOK:(dt + 1) * TOK]))
        p.wait(sy, "hw")
    return nc




def emit_ln(C, xT, xlnT, ncols=TOK):
    nc, p = C["nc"], C["p"]
    v, sc, t = C["v"], C["sc"], C["t"]
    ps, ones1_f, onescol, eps_sb = C["ps"], C["ones1_f"], C["onescol"], C["eps_sb"]
    if "ln_sq" not in C:
        C["ln_sq"] = C["al"]("ln_sq", [128, 512], F32)
        C["ln_m"] = C["al"]("ln_m", [1, 512], F32)
        C["ln_e"] = C["al"]("ln_e", [1, 512], F32)
        C["ln_s"] = C["al"]("ln_s", [1, 512], F32)
        C["ln_a"] = C["al"]("ln_a", [1, 512], F32)
        C["ln_b"] = C["al"]("ln_b", [1, 512], F32)
        C["ln_bc"] = C["al"]("ln_bc", [128, 1024], F32)
    sq, mrow, erow, srow = C["ln_sq"], C["ln_m"], C["ln_e"], C["ln_s"]
    arow, brow, ab_bc = C["ln_a"], C["ln_b"], C["ln_bc"]
    g = C["g"]
    p.join([C["v"], C["g"], C["sc"], C["t"]])
    for ch in range(ncols // 512):
        p.join([sc, t])
        pe_sq = 0
        for dt in range(DT):
            xs = xT[:, dt * ncols + ch * 512: dt * ncols + ch * 512 + 512]
            if dt:
                p.wait(sc, "pe", pe_sq)
            p.inc("act", sc.square(sq[:], xs))
            p.wait(t, "act")
            p.inc("pe", t.matmul(ps[5][0:1, :], onescol[:], xs,
                                 start=(dt == 0), stop=(dt == DT - 1)))
            pe_sq = p.inc("pe", t.matmul(ps[6][0:1, :], onescol[:], sq[:],
                                         start=(dt == 0), stop=(dt == DT - 1)))
        p.wait(v, "pe")
        p.inc("dve", v.tensor_scalar_mul(mrow[:], ps[5][0:1, :], 1.0 / D))
        p.inc("dve", v.tensor_scalar_mul(erow[:], ps[6][0:1, :], 1.0 / D))
        p.wait(v, "dve")
        p.inc("dve", v.tensor_mul(srow[:], mrow[:], mrow[:]))
        p.wait(v, "dve")
        p.inc("dve", v.tensor_sub(srow[:], erow[:], srow[:]))
        p.wait(sc, "dve")
        p.inc("act", sc.activation(srow[:], srow[:], AF.Sqrt, bias=eps_sb[:]))
        p.wait(v, "act")
        p.inc("dve", v.reciprocal(arow[:], srow[:]))
        p.wait(v, "dve")
        p.inc("dve", v.scalar_tensor_tensor(brow[:], mrow[:], -1.0, arow[:],
                                            ALU.mult, ALU.mult))
        p.wait(t, "dve")
        p.inc("pe", t.matmul(ps[5][:], ones1_f[:], arow[:], start=True, stop=True))
        p.inc("pe", t.matmul(ps[6][:], ones1_f[:], brow[:], start=True, stop=True))
        p.wait(sc, "pe")
        p.inc("act", sc.copy(ab_bc[:, 0:512], ps[5][:]))
        p.inc("act", sc.copy(ab_bc[:, 512:1024], ps[6][:]))
        p.wait(g, "act")
        p.wait(v, "act")
        for dt in range(DT):
            xs = xT[:, dt * ncols + ch * 512: dt * ncols + ch * 512 + 512]
            xd = xlnT[:, dt * ncols + ch * 512: dt * ncols + ch * 512 + 512]
            if dt >= 2:
                p.wait(g, "dve")
            p.inc("pool", g.tensor_mul(xd_f32(C, dt), xs, ab_bc[:, 0:512]))
            p.wait(v, "pool")
            p.inc("dve", v.tensor_add(xd, xd_f32(C, dt), ab_bc[:, 512:1024]))


def xd_f32(C, dt):
    if "ln_t" not in C:
        C["ln_t"] = C["al"]("ln_t", [128, 1024], F32)
    return C["ln_t"][:, (dt % 2) * 512:(dt % 2) * 512 + 512]


def emit_ffn_l0(C, xT, xlnT, w1_d, b1_d, w2_d, b2_d, wr_d, br_d, dbg_t):
    nc, p = C["nc"], C["p"]
    v, sc, g, t, sy = C["v"], C["sc"], C["g"], C["t"], C["sy"]
    ps, ones1_f, ident_f = C["ps"], C["ones1_f"], C["ident_f"]
    fstack = contextlib.ExitStack()
    al = lambda n, sh, dt_: fstack.enter_context(nc.sbuf_tensor(n, sh, dt_))

    wr_bf = al("wr_bf", [128, DT * 8], BF16)
    br_bc = al("br_bc", [128, 8], F32)
    br_row = al("br_row", [1, 8], F32)
    sc8 = al("sc8", [128, 8], F32)
    eq1 = al("eq1", [128, 8], F32)
    eq2 = al("eq2", [128, 8], F32)
    sm8 = al("sm8", [128, 8], F32)
    m1c = al("m1c", [128, 1], F32)
    m2c = al("m2c", [128, 1], F32)
    w1c = al("w1c", [128, 1], F32)
    w2c = al("w2c", [128, 1], F32)
    wt_all = al("wt_all", [128, 16 * 8], F32)
    wrow_s = al("wrow_s", [1, 512], F32)
    wrow_b = al("wrow_b", [1, 512], BF16)
    wbc = al("wbc", [128, 512], F32)

    for dt in range(DT):
        p.inc16("gd", g.dma_start(out=wr_bf[:, dt * 8:dt * 8 + 8],
                                  in_=wr_d[128 * dt:128 * dt + 128, :]))
    p.inc16("hw", sy.dma_start(out=br_row[:], in_=br_d))
    p.join([t, v, sc])
    p.inc("pe", t.matmul(ps[4][:, 0:8], ones1_f[:], br_row[:], start=True, stop=True))
    p.wait(sc, "pe")
    p.inc("act", sc.copy(br_bc[:], ps[4][:, 0:8]))
    p.wait(v, "act")
    for ti in range(16):
        psr = ps[ti % 2]
        p.join([t], sems=("dve", "act"))
        for dt in range(DT):
            p.inc("pe", t.matmul(psr[:, 0:8], xlnT[:, dt * TOK + ti * 128: dt * TOK + ti * 128 + 128],
                                 wr_bf[:, dt * 8:dt * 8 + 8],
                                 start=(dt == 0), stop=(dt == DT - 1)))
        p.wait(v, "pe")
        p.inc("dve", v.tensor_add(sc8[:], psr[:, 0:8], br_bc[:]))
        p.wait(v, "dve")
        p.inc("dve", v.reduce_max(m1c[:], sc8[:], axis=AX.X))
        p.wait(v, "dve")
        p.inc("dve", v.tensor_scalar(eq1[:], sc8[:], m1c[:], None, ALU.is_equal))
        p.wait(v, "dve")
        p.inc("dve", v.scalar_tensor_tensor(sm8[:], eq1[:], -1e30, sc8[:], ALU.mult, ALU.add))
        p.wait(v, "dve")
        p.inc("dve", v.reduce_max(m2c[:], sm8[:], axis=AX.X))
        p.wait(v, "dve")
        p.inc("dve", v.tensor_scalar(eq2[:], sm8[:], m2c[:], None, ALU.is_equal))
        p.inc("dve", v.tensor_sub(w1c[:], m2c[:], m1c[:]))
        p.wait(sc, "dve")
        p.inc("act", sc.activation(w1c[:], w1c[:], AF.Exp))
        p.wait(v, "act")
        p.inc("dve", v.tensor_scalar(w2c[:], w1c[:], 1.0, None, ALU.add))
        p.wait(v, "dve")
        p.inc("dve", v.reciprocal(w2c[:], w2c[:]))
        p.wait(v, "dve")
        p.inc("dve", v.tensor_mul(w1c[:], w1c[:], w2c[:]))
        p.wait(v, "dve")
        p.inc("dve", v.tensor_scalar(eq1[:], eq1[:], w2c[:], None, ALU.mult))
        p.wait(v, "dve")
        p.inc("dve", v.scalar_tensor_tensor(wt_all[:, ti * 8:ti * 8 + 8],
                                            eq2[:], w1c[:], eq1[:], ALU.mult, ALU.add))
    if "wt" in dbg_t:
        p.join([sy])
        p.inc16("hw", sy.dma_start(out=dbg_t["wt"], in_=wt_all[:]))
        p.wait(sy, "hw")

    w1s = al("w1s", [128, 2 * DT * 1024], BF16)
    w2s = al("w2s", [128, 2 * DT * 1024], BF16)
    b1T = al("b1T", [128, 32], F32)
    b2r = al("b2r", [1, D], BF16)
    htil = al("htil", [128, 8 * 512], BF16)

    def load_wslice(e, pg, buf):
        gsem = "gdf%d" % buf
        for dt in range(DT):
            p.inc16(gsem, g.dma_start(
                out=w1s[:, buf * DT * 1024 + dt * 1024: buf * DT * 1024 + (dt + 1) * 1024],
                in_=w1_d[e, 128 * dt:128 * dt + 128, 1024 * pg:1024 * pg + 1024]))
        for pt in range(DT):
            p.inc16(gsem, g.dma_start(
                out=w2s[:, buf * DT * 1024 + pt * 1024: buf * DT * 1024 + (pt + 1) * 1024],
                in_=w2_d[e, 1024 * pg + 128 * pt:1024 * pg + 128 * pt + 128, :]))
        return p.n(gsem)

    steps = [(e, pg) for e in range(8) for pg in range(4)]
    p.join([g])
    gd_done = {}
    gd_done[0] = load_wslice(0, 0, 0)
    pe_use = [0, 0]
    for i, (e, pg) in enumerate(steps):
        buf = i % 2
        if i + 1 < len(steps):
            if i >= 1:
                p.wait(g, "pe", pe_use[(i + 1) % 2])
            gd_done[i + 1] = load_wslice(*steps[i + 1], (i + 1) % 2)
        p.wait(t, "gdf%d" % buf, gd_done[i])
        if pg == 0:
            p.join([sy], sems=("pe", "act"))
            p.inc16("hw", sy.dma_start(out=C["ln_sq"][0:32, 0:128],
                                       in_=b1_d[e:e + 1, :].rearrange("o (c p) -> (o c) p", p=128)))
            p.inc16("gdb", g.dma_start(out=b2r[:], in_=b2_d[e:e + 1, :]))
            p.wait(t, "hw")
            p.inc("pe", t.transpose(ps[6][:, 0:32], C["ln_sq"][0:32, 0:128], ident_f[0:32, 0:32]))
            p.wait(sc, "pe")
            p.inc("act", sc.copy(b1T[:], ps[6][:, 0:32]))
        wbase = buf * DT * 1024
        for qb in range(4):
            p.join([t], sems=("act", "dve"))
            for sl in range(4):
                ti = qb * 4 + sl
                p.inc("pe", t.transpose(ps[6][0:1, sl * 128:sl * 128 + 128],
                                        wt_all[:, ti * 8 + e:ti * 8 + e + 1], ident_f[:]))
            p.wait(sc, "pe")
            p.wait(sc, "gdb")
            p.inc("act", sc.copy(wrow_s[:], ps[6][0:1, 0:512]))
            p.wait(v, "act")
            p.inc("dve", v.tensor_copy(wrow_b[:], wrow_s[:]))
            p.wait(t, "act")
            p.inc("pe", t.matmul(ps[4][:], ones1_f[:], wrow_s[:], start=True, stop=True))
            p.wait(sc, "pe")
            act_wbc = p.inc("act", sc.copy(wbc[:], ps[4][:]))
            p.wait(v, "act", act_wbc)
            dve_h = 0
            for pi in range(8):
                psh = ps[pi % 2]
                if pi >= 2:
                    p.wait(t, "dve", dve_h)
                for dt in range(DT):
                    p.inc("pe", t.matmul(psh[:], w1s[:, wbase + dt * 1024 + 128 * pi: wbase + dt * 1024 + 128 * pi + 128],
                                         xlnT[:, dt * TOK + qb * 512: dt * TOK + qb * 512 + 512],
                                         start=(dt == 0), stop=(dt == DT - 1)))
                p.wait(sc, "pe")
                p.inc("act", sc.activation(C["ln_t"][:, 0:512] if False else htil[:, pi * 512:pi * 512 + 512],
                                           psh[:], AF.Gelu_apprx_tanh,
                                           bias=b1T[:, pg * 8 + pi:pg * 8 + pi + 1]))
                p.wait(v, "act")
                dve_h = p.inc("dve", v.tensor_mul(htil[:, pi * 512:pi * 512 + 512],
                                                  htil[:, pi * 512:pi * 512 + 512], wbc[:]))
            p.wait(t, "dve", dve_h)
            dve_y = 0
            for dt in range(DT):
                psy = ps[2 + dt % 2]
                if dt >= 2:
                    p.wait(t, "dve", dve_y)
                for pi in range(8):
                    p.inc("pe", t.matmul(psy[:], w2s[:, wbase + pi * 1024 + 128 * dt: wbase + pi * 1024 + 128 * dt + 128],
                                         htil[:, pi * 512:pi * 512 + 512],
                                         start=(pi == 0), stop=(pi == 7 and pg != 0)))
                if pg == 0:
                    p.inc("pe", t.matmul(psy[:], b2r[0:1, 128 * dt:128 * dt + 128],
                                         wrow_b[:], start=False, stop=True))
                p.wait(v, "pe")
                dve_y = p.inc("dve", v.tensor_add(
                    xT[:, dt * TOK + qb * 512: dt * TOK + qb * 512 + 512],
                    xT[:, dt * TOK + qb * 512: dt * TOK + qb * 512 + 512], psy[:]))
        pe_use[buf] = p.n("pe")
    p.join([v, sc, g, t, sy])
    fstack.close()


def emit_cqkv(C, xlnT, wdq_d, wdkv_d, cq_o, ckv_o):
    nc, p = C["nc"], C["p"]
    v, sc, g, t, sy = C["v"], C["sc"], C["g"], C["t"], C["sy"]
    ps = C["ps"]
    al = C["al"]
    wd_bf = al("wd_bf", [128, 2 * DT * 256], BF16)
    cstage = al("cstage", [128, 1024], BF16)
    for dt in range(DT):
        p.inc16("gdc", g.dma_start(out=wd_bf[:, dt * 256:dt * 256 + 256],
                                   in_=wdq_d[128 * dt:128 * dt + 128, :]))
        p.inc16("gdc", g.dma_start(out=wd_bf[:, DT * 256 + dt * 256: DT * 256 + dt * 256 + 256],
                                   in_=wdkv_d[128 * dt:128 * dt + 128, :]))
    p.join([t, sy])
    for wi, outd in ((0, cq_o), (1, ckv_o)):
        for c0 in range(2):
            for qb in range(4):
                i = wi * 8 + c0 * 4 + qb
                pc = ps[i % 2]
                p.wait(t, "act")
                if i >= 2:
                    p.wait(sc, "hw", hw_c[i % 2])
                for dt in range(DT):
                    p.inc("pe", t.matmul(
                        pc[:], wd_bf[:, wi * DT * 256 + dt * 256 + 128 * c0: wi * DT * 256 + dt * 256 + 128 * c0 + 128],
                        xlnT[:, dt * TOK + qb * 512: dt * TOK + qb * 512 + 512],
                        start=(dt == 0), stop=(dt == DT - 1)))
                p.wait(sc, "pe")
                stg = cstage[:, (i % 2) * 512:(i % 2) * 512 + 512]
                p.inc("act", sc.copy(stg, pc[:]))
                p.wait(sy, "act")
                if i < 2:
                    hw_c = [0, 0] if i == 0 else hw_c
                hw_c[i % 2] = p.inc16("hw", sy.dma_start(
                    out=outd[128 * c0:128 * c0 + 128, qb * 512:qb * 512 + 512], in_=stg))
                p.wait(sy, "hw")
    p.wait(sy, "hw")




def emit_bounds(C, qmat, qcols, kmat, kcols, nrm, bvals, negb, sqb, scale):
    p = C["p"]
    v, sc, t = C["v"], C["sc"], C["t"]
    ps, ones1_f = C["ps"], C["ones1_f"]
    halfsel = C["halfsel_bf"] if sqb.dtype == BF16 else C["halfsel"]
    srow = C["ln_s"]
    p.join([sc, t, v])
    p.inc("dve", v.memset(nrm[:], 0.0))
    for (mat, ncols, col0) in ((qmat, qcols, 0), (kmat, kcols, 2)):
        p.wait(sc, "pe")
        p.inc("act", sc.square(sqb[:, 0:ncols], mat[:, 0:ncols]))
        p.wait(t, "act")
        p.wait(v, "dve")
        for ch in range(ncols // 512):
            for hh in range(2):
                p.inc("pe", t.matmul(ps[6][0:1, :], halfsel[:, hh:hh + 1],
                                     sqb[:, ch * 512:ch * 512 + 512],
                                     start=True, stop=True))
                p.wait(v, "pe")
                p.inc("dve", v.reduce_max(srow[0:1, 0:1], ps[6][0:1, :], axis=AX.X))
                p.wait(v, "dve")
                p.inc("dve", v.tensor_max(nrm[0:1, col0 + hh:col0 + hh + 1],
                                          nrm[0:1, col0 + hh:col0 + hh + 1],
                                          srow[0:1, 0:1]))
                p.wait(t, "dve")
    p.wait(sc, "dve")
    p.inc("act", sc.activation(nrm[0:1, 4:8], nrm[0:1, 0:4], AF.Sqrt))
    p.wait(v, "act")
    p.inc("dve", v.scalar_tensor_tensor(bvals[0:1, 0:2], nrm[0:1, 4:6], -scale,
                                        nrm[0:1, 6:8], ALU.mult, ALU.mult))
    p.wait(t, "dve")
    p.inc("pe", t.matmul(ps[6][:, 0:2], ones1_f[:], bvals[0:1, 0:2], start=True, stop=True))
    p.wait(sc, "pe")
    p.inc("act", sc.copy(negb[:], ps[6][:, 0:2]))


def build_launch2(dbg=()):
    nc = bass.Bass("TRN2", target_bir_lowering=False, num_devices=8)
    p = P(nc)
    v, sc, g, t, sy = nc.vector, nc.scalar, nc.gpsimd, nc.tensor, nc.sync

    x_inT = nc.dram_tensor("x_inT", [D, TOK], F32, kind="ExternalInput").ap()
    cq_i = nc.dram_tensor("cq_i", [256, TOK], BF16, kind="ExternalInput").ap()
    ckv_i = nc.dram_tensor("ckv_i", [256, S], BF16, kind="ExternalInput").ap()
    wuq_d = nc.dram_tensor("wuq_d", [256, D], F32, kind="ExternalInput").ap()
    wuk_d = nc.dram_tensor("wuk_d", [256, D], F32, kind="ExternalInput").ap()
    wuv_d = nc.dram_tensor("wuv_d", [256, D], F32, kind="ExternalInput").ap()
    wo1_d = nc.dram_tensor("wo1_d", [D, D], F32, kind="ExternalInput").ap()
    qpos_d = nc.dram_tensor("qpos_d", [1, TOK], F32, kind="ExternalInput").ap()
    kpos_d = nc.dram_tensor("kpos_d", [128, 32], F32, kind="ExternalInput").ap()
    wg_d = nc.dram_tensor("wg_d", [8, D], F32, kind="ExternalInput").ap()
    eb_d = nc.dram_tensor("eb_d", [1, 8], F32, kind="ExternalInput").ap()
    mw1_d = nc.dram_tensor("mw1_d", [8, D, 512], F32, kind="ExternalInput").ap()
    mw2_d = nc.dram_tensor("mw2_d", [8, 512, D], F32, kind="ExternalInput").ap()
    mw3_d = nc.dram_tensor("mw3_d", [8, D, 512], F32, kind="ExternalInput").ap()
    sw1_d = nc.dram_tensor("sw1_d", [D, D], F32, kind="ExternalInput").ap()
    sw2_d = nc.dram_tensor("sw2_d", [D, D], F32, kind="ExternalInput").ap()
    sw3_d = nc.dram_tensor("sw3_d", [D, D], F32, kind="ExternalInput").ap()

    xo2_T = nc.dram_tensor("xo2_T", [D, TOK], F32, kind="ExternalOutput").ap()
    a0_o = nc.dram_tensor("a0_o", [128, 128], F32, kind="ExternalOutput").ap()
    a1_o = nc.dram_tensor("a1_o", [128, 128], F32, kind="ExternalOutput").ap()
    xf8_o = nc.dram_tensor("xf8_o", [D, 8], F32, kind="ExternalOutput").ap()
    dbg_t = {}
    if dbg and "o1" in dbg:
        dbg_t["o1"] = nc.dram_tensor("dbg_o1", [D, TOK], F32, kind="ExternalOutput").ap()
        dbg_t["xat"] = nc.dram_tensor("dbg_xat", [D, TOK], F32, kind="ExternalOutput").ap()
    if dbg and "wt1" in dbg:
        dbg_t["wt1"] = nc.dram_tensor("dbg_wt1", [128, 128], F32, kind="ExternalOutput").ap()
        dbg_t["xln3"] = nc.dram_tensor("dbg_xln3", [D, TOK], F32, kind="ExternalOutput").ap()

    ident_bf = nc.alloc_sbuf_tensor("ident_bf", [128, 128], BF16)
    ident_f = nc.alloc_sbuf_tensor("ident_f", [128, 128], F32)
    ones1_f = nc.alloc_sbuf_tensor("ones1_f", [1, 128], F32)
    onescol = nc.alloc_sbuf_tensor("onescol", [128, 1], F32)
    halfsel = nc.alloc_sbuf_tensor("halfsel", [128, 2], F32)
    halfsel_bf = nc.alloc_sbuf_tensor("halfsel_bf", [128, 2], BF16)
    eps_sb = nc.alloc_sbuf_tensor("eps_sb", [1, 1], F32)
    v.memset(ident_bf[:], 1.0)
    v.memset(ident_f[:], 1.0)
    v.memset(ones1_f[:], 1.0)
    v.memset(onescol[:], 1.0)
    v.memset(eps_sb[:], EPS)
    p.inc("dve", v.memset(halfsel[:], 0.0))
    p.inc("dve", v.memset(halfsel_bf[:], 0.0))
    p.wait(v, "dve")
    v.memset(halfsel[0:64, 0:1], 1.0)
    v.memset(halfsel_bf[0:64, 0:1], 1.0)
    v.memset(halfsel_bf[64:128, 1:2], 1.0)
    p.inc("dve", v.memset(halfsel[64:128, 1:2], 1.0))
    p.wait(g, "dve")
    p.inc("pool", g.affine_select(ident_bf[:], ident_bf[:], [[-1, 128]],
                                  ALU.is_equal, 0.0, base=0, channel_multiplier=1))
    p.inc("pool", g.affine_select(ident_f[:], ident_f[:], [[-1, 128]],
                                  ALU.is_equal, 0.0, base=0, channel_multiplier=1))

    ps = [nc.alloc_psum_tensor(f"ps{i}", [128, 512], F32) for i in range(7)]
    ps_tr = nc.alloc_psum_tensor("ps_tr", [128, 128], BF16)

    xT = nc.alloc_sbuf_tensor("xT", [128, DT * TOK], F32)
    for dt in range(DT):
        p.inc16("hw", sy.dma_start(out=xT[:, dt * TOK:(dt + 1) * TOK],
                                   in_=x_inT[128 * dt:128 * dt + 128, :]))
    p.wait(sy, "hw")

    C = dict(nc=nc, p=p, v=v, sc=sc, g=g, t=t, sy=sy, ps=ps, ps_tr=ps_tr,
             ones1_f=ones1_f, onescol=onescol, eps_sb=eps_sb, ident_bf=ident_bf,
             ident_f=ident_f, halfsel=halfsel, halfsel_bf=halfsel_bf)

    with contextlib.ExitStack() as stack:
        al = lambda n, sh, dt_: stack.enter_context(nc.sbuf_tensor(n, sh, dt_))
        C["al"] = al
        C["ln_s"] = al("ln_s0", [1, 512], F32)
        cqT = al("cqT", [128, 2 * TOK], BF16)
        ckvT = al("ckvT", [128, 2 * S], BF16)
        wu_bf = al("wu_bf", [128, 3 * 2 * D], BF16)
        qpos = al("qpos", [1, TOK], F32)
        kposT = al("kposT", [128, 32], F32)
        qpos2 = al("qpos2", [128, 1024], F32)
        k2 = al("k2", [128, S], BF16)
        q_ab = al("q_ab", [128, TOK], BF16)
        vab = al("vab", [128, 2 * 32 * 65], BF16)
        pt = al("pt", [128, 2 * 1024], BF16)
        sqb = al("sqb", [128, S], BF16)
        nrm = al("nrm", [1, 8], F32)
        bvals = al("bvals", [1, 2], F32)
        negb = al("negb", [128, 2], F32)
        rrow = al("rrow", [1, 1024], F32)
        rb_sb = al("rb_sb", [128, 1024], F32)
        o1T = al("o1T", [128, DT * TOK], BF16)
        wo1_bf = al("wo1_bf", [128, DT * D], BF16)

        for c0 in range(2):
            p.inc16("gd", g.dma_start(out=cqT[:, c0 * TOK:(c0 + 1) * TOK],
                                      in_=cq_i[128 * c0:128 * c0 + 128, :]))
            p.inc16("gd", g.dma_start(out=ckvT[:, c0 * S:(c0 + 1) * S],
                                      in_=ckv_i[128 * c0:128 * c0 + 128, :]))
        for wi, wd in enumerate((wuq_d, wuk_d, wuv_d)):
            for c0 in range(2):
                p.inc16("gd", g.dma_start(out=wu_bf[:, wi * 2 * D + c0 * D: wi * 2 * D + (c0 + 1) * D],
                                          in_=wd[128 * c0:128 * c0 + 128, :]))
        p.inc16("hw", sy.dma_start(out=qpos[:], in_=qpos_d))
        p.inc16("hw", sy.dma_start(out=kposT[:], in_=kpos_d))
        p.wait(sy, "hw")
        p.wait(t, "gd")
        p.wait(t, "hw")

        for hp in range(8):
            p.join([t], sems=("act", "dve", "pool"))
            for kb in range(8):
                pk = ps[kb % 2]
                if kb >= 2:
                    p.wait(t, "act")
                for hh in range(2):
                    for c0 in range(2):
                        p.inc("pe", t.matmul(
                            pk[64 * hh:64 * hh + 64, :],
                            wu_bf[:, 2 * D + c0 * D + 64 * (2 * hp + hh): 2 * D + c0 * D + 64 * (2 * hp + hh) + 64],
                            ckvT[:, c0 * S + kb * 512: c0 * S + kb * 512 + 512],
                            start=(c0 == 0), stop=(c0 == 1),
                            tile_position=(0, 64 * hh)))
                p.wait(sc, "pe")
                p.inc("act", sc.copy(k2[:, kb * 512:kb * 512 + 512], pk[:]))
            for qb in range(4):
                pk = ps[qb % 2]
                p.wait(t, "act")
                for hh in range(2):
                    for c0 in range(2):
                        p.inc("pe", t.matmul(
                            pk[64 * hh:64 * hh + 64, :],
                            wu_bf[:, c0 * D + 64 * (2 * hp + hh): c0 * D + 64 * (2 * hp + hh) + 64],
                            cqT[:, c0 * TOK + qb * 512: c0 * TOK + qb * 512 + 512],
                            start=(c0 == 0), stop=(c0 == 1),
                            tile_position=(0, 64 * hh)))
                p.wait(sc, "pe")
                p.inc("act", sc.copy(q_ab[:, qb * 512:qb * 512 + 512], pk[:]))
            p.wait(v, "act")
            p.inc("dve", v.memset(vab[:], 1.0))
            p.wait(t, "dve")
            for kt in range(32):
                pk = ps[2 + kt % 2]
                if kt >= 2:
                    p.wait(t, "act")
                for c0 in range(2):
                    p.inc("pe", t.matmul(
                        pk[:, 0:128],
                        ckvT[:, c0 * S + kt * 128: c0 * S + kt * 128 + 128],
                        wu_bf[:, 2 * 2 * D + c0 * D + 128 * hp: 2 * 2 * D + c0 * D + 128 * hp + 128],
                        start=(c0 == 0), stop=(c0 == 1)))
                p.wait(sc, "pe")
                dst = vab[:, :].rearrange("p (hh c) -> p hh c", hh=2)[:, :, kt * 65:kt * 65 + 64]
                srcp = pk[:, 0:128].rearrange("p (hh c) -> p hh c", hh=2)
                p.inc("act", sc.copy(dst, srcp))
            emit_bounds(C, q_ab, TOK, k2, S, nrm, bvals, negb, sqb, 0.125)
            p.join([t, sc, v, g])
            for w in range(4):
                p.wait(t, "dve")
                p.inc("pe", t.matmul(ps[5][:], ones1_f[:],
                                     qpos[0:1, w * 512:w * 512 + 512], start=True, stop=True))
                p.inc("pe", t.matmul(ps[6][:], ones1_f[:],
                                     qpos[0:1, w * 512:w * 512 + 512], start=True, stop=True))
                p.wait(sc, "pe")
                p.inc("act", sc.copy(qpos2[:, 0:512], ps[5][:]))
                p.inc("act", sc.copy(qpos2[:, 512:1024], ps[6][:]))
                p.wait(v, "act")
                slots = list(range(0, 2 * w + 2)) + list(range(8, 8 + 2 * w + 2))
                kts = [2 * s + z for s in slots for z in (0, 1)]
                masked = {2 * (2 * w), 2 * (2 * w) + 1, 2 * (2 * w + 1), 2 * (2 * w + 1) + 1,
                          2 * (8 + 2 * w), 2 * (8 + 2 * w) + 1, 2 * (8 + 2 * w + 1), 2 * (8 + 2 * w + 1) + 1}
                exp_done = [0, 0]
                pv_done = [0, 0]
                msk_done = [0, 0]
                for ki, kt in enumerate(kts):
                    r = ki % 2
                    psA, psB = ps[0 + r], ps[2 + r]
                    ptbuf = pt[:, r * 1024:(r + 1) * 1024]
                    if exp_done[r]:
                        p.wait(t, "act", exp_done[r])
                    p.inc("pe", t.matmul(psA[:], k2[0:64, 128 * kt:128 * kt + 128],
                                         q_ab[0:64, w * 512:w * 512 + 512],
                                         start=True, stop=True, tile_position=(0, 0)))
                    pe_s = p.inc("pe", t.matmul(psB[:], k2[64:128, 128 * kt:128 * kt + 128],
                                                q_ab[64:128, w * 512:w * 512 + 512],
                                                start=True, stop=True, tile_position=(64, 0)))
                    p.wait(sc, "pe", pe_s)
                    if pv_done[r]:
                        p.wait(sc, "pe", pv_done[r])
                    if msk_done[r]:
                        p.wait(sc, "dve", msk_done[r])
                    p.inc("act", sc.activation(ptbuf[:, 0:512], psA[:], AF.Exp,
                                               bias=negb[:, 0:1], scale=0.125))
                    exp_done[r] = p.inc("act", sc.activation(ptbuf[:, 512:1024], psB[:], AF.Exp,
                                                             bias=negb[:, 1:2], scale=0.125))
                    if kt in masked:
                        p.wait(v, "act", exp_done[r])
                        msk_done[r] = p.inc("dve", v.scalar_tensor_tensor(
                            ptbuf[:], qpos2[:], kposT[:, kt:kt + 1], ptbuf[:],
                            ALU.is_ge, ALU.mult))
                        p.wait(t, "dve", msk_done[r])
                    else:
                        p.wait(t, "act", exp_done[r])
                    vA = vab[:, 65 * kt:65 * kt + 65]
                    vB = vab[:, 2080 + 65 * kt:2080 + 65 * kt + 65]
                    p.inc("pe", t.matmul(ps[4][0:65, :], vA, ptbuf[:, 0:512],
                                         start=(ki == 0), stop=(ki == len(kts) - 1)))
                    pv_done[r] = p.inc("pe", t.matmul(ps[5][0:65, :], vB, ptbuf[:, 512:1024],
                                                      start=(ki == 0), stop=(ki == len(kts) - 1)))
                p.wait(v, "pe")
                p.inc("dve", v.reciprocal(rrow[0:1, 0:512], ps[4][64:65, :]))
                p.inc("dve", v.reciprocal(rrow[0:1, 512:1024], ps[5][64:65, :]))
                p.wait(t, "dve")
                p.inc("pe", t.matmul(ps[6][:], ones1_f[:], rrow[0:1, 0:512], start=True, stop=True))
                p.wait(sc, "pe")
                p.inc("act", sc.copy(rb_sb[:, 0:512], ps[6][:]))
                p.wait(t, "act")
                p.inc("pe", t.matmul(ps[6][:], ones1_f[:], rrow[0:1, 512:1024], start=True, stop=True))
                p.wait(sc, "pe")
                p.inc("act", sc.copy(rb_sb[:, 512:1024], ps[6][:]))
                p.wait(v, "act")
                for hh in range(2):
                    pso = (ps[4], ps[5])[hh]
                    p.inc("dve", v.tensor_tensor(
                        o1T[64 * hh:64 * hh + 64, hp * TOK + w * 512: hp * TOK + w * 512 + 512],
                        pso[0:64, :], rb_sb[0:64, hh * 512:(hh + 1) * 512], ALU.mult))
                p.wait(t, "dve")

        for ct in range(DT):
            p.inc16("gdw0", g.dma_start(out=wo1_bf[:, ct * D:(ct + 1) * D],
                                        in_=wo1_d[128 * ct:128 * ct + 128, :]))
        p.join([t, v])
        dve_wo = [0, 0]
        for i, (dt, qb) in enumerate([(a, b_) for a in range(DT) for b_ in range(4)]):
            r = i % 2
            pw = ps[r]
            if i >= 2:
                p.wait(t, "dve", dve_wo[r])
            for ct in range(DT):
                p.inc("pe", t.matmul(pw[:], wo1_bf[:, ct * D + 128 * dt: ct * D + 128 * dt + 128],
                                     o1T[:, ct * TOK + qb * 512: ct * TOK + qb * 512 + 512],
                                     start=(ct == 0), stop=(ct == DT - 1)))
            p.wait(v, "pe")
            dve_wo[r] = p.inc("dve", v.tensor_add(
                xT[:, dt * TOK + qb * 512: dt * TOK + qb * 512 + 512],
                xT[:, dt * TOK + qb * 512: dt * TOK + qb * 512 + 512], pw[:]))

        if "o1" in dbg_t:
            p.join([g, sy])
            for dt in range(DT):
                p.inc16("gd", g.dma_start(out=dbg_t["o1"][128 * dt:128 * dt + 128, :],
                                          in_=o1T[:, dt * TOK:(dt + 1) * TOK]))
                p.inc16("hw", sy.dma_start(out=dbg_t["xat"][128 * dt:128 * dt + 128, :],
                                           in_=xT[:, dt * TOK:(dt + 1) * TOK]))
            p.wait(sy, "hw")
            p.wait(g, "gd")

    with contextlib.ExitStack() as stack:
        al = lambda n, sh, dt_: stack.enter_context(nc.sbuf_tensor(n, sh, dt_))
        C["al"] = al
        for kk in ("ln_sq", "ln_m", "ln_e", "ln_s", "ln_a", "ln_b", "ln_bc", "ln_t"):
            C.pop(kk, None)
        xlnT = al("xlnT", [128, DT * TOK], BF16)
        C["a0_sb"] = al("a0_sb", [128, 128], F32)
        C["a1_sb"] = al("a1_sb", [128, 128], F32)
        emit_ln(C, xT, xlnT)
        p.join([g])
        for dt in range(DT):
            p.inc16("gd", g.dma_start(out=xf8_o[128 * dt:128 * dt + 128, :],
                                      in_=xlnT[:, dt * TOK:dt * TOK + 8]))
        p.wait(g, "gd")
        if "xln3" in dbg_t:
            p.join([g])
            for dt in range(DT):
                p.inc16("gd", g.dma_start(out=dbg_t["xln3"][128 * dt:128 * dt + 128, :],
                                          in_=xlnT[:, dt * TOK:(dt + 1) * TOK]))
            p.wait(g, "gd")
        emit_ffn_l1(C, xT, xlnT, wg_d, eb_d, mw1_d, mw2_d, mw3_d, sw1_d, sw2_d, sw3_d)
        p.join([sy])
        p.inc16("hw", sy.dma_start(out=a0_o, in_=C["a0_sb"][:]))
        p.inc16("hw", sy.dma_start(out=a1_o, in_=C["a1_sb"][:]))
        p.wait(sy, "hw")

        p.join([sy])
        for dt in range(DT):
            p.inc16("hw", sy.dma_start(out=xo2_T[128 * dt:128 * dt + 128, :],
                                       in_=xT[:, dt * TOK:(dt + 1) * TOK]))
        p.wait(sy, "hw")
    return nc


def emit_ffn_l1(C, xT, xlnT, wg_d, eb_d, mw1_d, mw2_d, mw3_d, sw1_d, sw2_d, sw3_d):
    nc, p = C["nc"], C["p"]
    v, sc, g, t, sy = C["v"], C["sc"], C["g"], C["t"], C["sy"]
    ps, ones1_f, ident_f, ident_bf = C["ps"], C["ones1_f"], C["ident_f"], C["ident_bf"]
    fstack = contextlib.ExitStack()
    al = lambda n, sh, dt_: fstack.enter_context(nc.sbuf_tensor(n, sh, dt_))

    wg_bf = al("wg_bf", [128, DT * 8], BF16)
    wg_raw = al("wg_raw", [8, D], BF16)
    eb_bc = al("eb_bc", [128, 8], F32)
    eb_row = al("eb_row", [1, 8], F32)
    p8 = al("p8", [128, 8], F32)
    sc8 = al("sc8", [128, 8], F32)
    g4 = al("g4", [128, 4], F32)
    eq1 = al("eq1", [128, 8], F32)
    eq2 = al("eq2", [128, 8], F32)
    sm8 = al("sm8", [128, 8], F32)
    keep8 = al("keep8", [128, 8], F32)
    m1c = al("m1c", [128, 1], F32)
    m2c = al("m2c", [128, 1], F32)
    ssum = al("ssum", [128, 1], F32)
    a0_sb, a1_sb = C["a0_sb"], C["a1_sb"]

    p.join([g, sy, t])
    p.inc16("gd", g.dma_start(out=wg_raw[:], in_=wg_d))
    p.inc16("hw", sy.dma_start(out=eb_row[:], in_=eb_d))
    p.wait(t, "gd")
    p.wait(t, "hw")
    for dt in range(DT):
        if dt:
            p.wait(t, "act")
        p.inc("pe", t.transpose(ps_tr_slice(C, 8), wg_raw[:, 128 * dt:128 * dt + 128],
                                ident_bf[0:8, 0:8]))
        p.wait(sc, "pe")
        p.inc("act", sc.copy(wg_bf[:, dt * 8:dt * 8 + 8], ps_tr_slice(C, 8)))
    p.inc("pe", t.matmul(ps[4][:, 0:8], ones1_f[:], eb_row[:], start=True, stop=True))
    p.wait(sc, "pe")
    p.inc("act", sc.copy(eb_bc[:], ps[4][:, 0:8]))
    p.join([v, t])
    for ti in range(16):
        psr = ps[ti % 2]
        p.join([t], sems=("dve", "act"))
        for dt in range(DT):
            p.inc("pe", t.matmul(psr[:, 0:8], xlnT[:, dt * TOK + ti * 128: dt * TOK + ti * 128 + 128],
                                 wg_bf[:, dt * 8:dt * 8 + 8],
                                 start=(dt == 0), stop=(dt == DT - 1)))
        p.wait(v, "pe")
        p.inc("dve", v.reduce_max(m1c[:], psr[:, 0:8], axis=AX.X, negate=True))
        p.wait(sc, "dve")
        p.wait(sc, "pe")
        p.inc("act", sc.activation(p8[:], psr[:, 0:8], AF.Exp, bias=m1c[:],
                                   accum_out=ssum[:]))
        p.wait(v, "act")
        p.inc("dve", v.reciprocal(ssum[:], ssum[:]))
        p.wait(v, "dve")
        p.inc("dve", v.tensor_scalar(p8[:], p8[:], ssum[:], None, ALU.mult))
        p.wait(v, "dve")
        p.inc("dve", v.tensor_add(sc8[:], p8[:], eb_bc[:]))
        p.wait(v, "dve")
        s2 = sc8[:].rearrange("p (g two) -> p g two", two=2)
        p.inc("dve", v.tensor_add(g4[:], s2[:, :, 0], s2[:, :, 1]))
        p.wait(v, "dve")
        p.inc("dve", v.reduce_max(m1c[:], g4[:], axis=AX.X))
        p.wait(v, "dve")
        p.inc("dve", v.tensor_scalar(eq1[:, 0:4], g4[:], m1c[:], None, ALU.is_equal))
        p.wait(v, "dve")
        p.inc("dve", v.scalar_tensor_tensor(sm8[:, 0:4], eq1[:, 0:4], -1e30, g4[:],
                                            ALU.mult, ALU.add))
        p.wait(v, "dve")
        p.inc("dve", v.reduce_max(m2c[:], sm8[:, 0:4], axis=AX.X))
        p.wait(v, "dve")
        p.inc("dve", v.tensor_scalar(eq2[:, 0:4], sm8[:, 0:4], m2c[:], None, ALU.is_equal))
        p.wait(v, "dve")
        p.inc("dve", v.tensor_add(eq1[:, 0:4], eq1[:, 0:4], eq2[:, 0:4]))
        p.wait(v, "dve")
        k2v = keep8[:].rearrange("p (g two) -> p g two", two=2)
        p.inc("dve", v.tensor_copy(k2v[:, :, 0], eq1[:, 0:4]))
        p.inc("dve", v.tensor_copy(k2v[:, :, 1], eq1[:, 0:4]))
        p.wait(v, "dve")
        p.inc("dve", v.tensor_scalar(keep8[:], keep8[:], -1.0, None, ALU.add))
        p.wait(v, "dve")
        p.inc("dve", v.scalar_tensor_tensor(sm8[:], keep8[:], 1e30, sc8[:],
                                            ALU.mult, ALU.add))
        p.wait(v, "dve")
        p.inc("dve", v.reduce_max(m1c[:], sm8[:], axis=AX.X))
        p.wait(v, "dve")
        p.inc("dve", v.tensor_scalar(eq1[:], sm8[:], m1c[:], None, ALU.is_equal))
        p.wait(v, "dve")
        p.inc("dve", v.scalar_tensor_tensor(sm8[:], eq1[:], -1e30, sm8[:], ALU.mult, ALU.add))
        p.wait(v, "dve")
        p.inc("dve", v.reduce_max(m2c[:], sm8[:], axis=AX.X))
        p.wait(v, "dve")
        p.inc("dve", v.tensor_scalar(eq2[:], sm8[:], m2c[:], None, ALU.is_equal))
        p.wait(v, "dve")
        p.inc("dve", v.tensor_mul(a0_sb[:, ti * 8:ti * 8 + 8], eq1[:], p8[:]))
        p.inc("dve", v.tensor_mul(a1_sb[:, ti * 8:ti * 8 + 8], eq2[:], p8[:]))

    swT = al("swT", [128, 3 * DT * D], BF16)
    sw_raw = al("sw_raw", [128, 2 * D], BF16)
    hshb = al("hshb", [128, 8 * 512], BF16)
    s1b = al("s1b", [128, 512], BF16)
    p.join([g, t, v])
    for wi, wd in enumerate((sw1_d, sw3_d, sw2_d)):
        for rt_ in range(DT):
            buf = sw_raw[:, (rt_ % 2) * D:(rt_ % 2) * D + D]
            gsem = "gdw%d" % (rt_ % 2)
            if rt_ >= 2 or wi:
                p.wait(g, "pe")
            p.inc16(gsem, g.dma_start(out=buf, in_=wd[128 * rt_:128 * rt_ + 128, :]))
            p.wait(t, gsem)
            for ct in range(DT):
                if wi or rt_ or ct:
                    p.wait(t, "dve")
                p.inc("pe", t.transpose(C["ps_tr"][:], buf[:, 128 * ct:128 * ct + 128], ident_bf[:]))
                p.wait(v, "pe")
                p.inc("dve", v.tensor_copy(
                    swT[:, wi * DT * D + rt_ * D + 128 * ct: wi * DT * D + rt_ * D + 128 * ct + 128],
                    C["ps_tr"][:]))
    p.join([t, sc, v])
    for qb in range(4):
        for st in range(DT):
            pu1, pu3 = ps[0 + st % 2], ps[2 + st % 2]
            if st >= 2 or qb:
                p.wait(t, "dve")
            for dt in range(DT):
                p.inc("pe", t.matmul(pu1[:], swT[:, 0 * DT * D + st * D + 128 * dt: 0 * DT * D + st * D + 128 * dt + 128],
                                     xlnT[:, dt * TOK + qb * 512: dt * TOK + qb * 512 + 512],
                                     start=(dt == 0), stop=(dt == DT - 1)))
            for dt in range(DT):
                p.inc("pe", t.matmul(pu3[:], swT[:, 1 * DT * D + st * D + 128 * dt: 1 * DT * D + st * D + 128 * dt + 128],
                                     xlnT[:, dt * TOK + qb * 512: dt * TOK + qb * 512 + 512],
                                     start=(dt == 0), stop=(dt == DT - 1)))
            p.wait(sc, "pe")
            p.wait(sc, "dve")
            p.inc("act", sc.activation(s1b[:], pu1[:], AF.Silu))
            p.wait(v, "act")
            p.inc("dve", v.tensor_mul(hshb[:, st * 512:st * 512 + 512], s1b[:], pu3[:]))
        p.wait(t, "dve")
        dve_sh = 0
        for dt in range(DT):
            py = ps[4 + dt % 2]
            if dt >= 2:
                p.wait(t, "dve", dve_sh)
            for st in range(DT):
                p.inc("pe", t.matmul(py[:], swT[:, 2 * DT * D + dt * D + 128 * st: 2 * DT * D + dt * D + 128 * st + 128],
                                     hshb[:, st * 512:st * 512 + 512],
                                     start=(st == 0), stop=(st == DT - 1)))
            p.wait(v, "pe")
            dve_sh = p.inc("dve", v.tensor_add(
                xT[:, dt * TOK + qb * 512: dt * TOK + qb * 512 + 512],
                xT[:, dt * TOK + qb * 512: dt * TOK + qb * 512 + 512], py[:]))
    p.join([v, sc, g, t, sy])
    for kk in ("swT", ):
        pass
    fstack.close()



def ps_tr_slice(C, n):
    return C["ps_tr"][:, 0:n]




def build_launch3():
    nc = bass.Bass("TRN2", target_bir_lowering=False, num_devices=8)
    p = P(nc)
    v, sc, g, t, sy = nc.vector, nc.scalar, nc.gpsimd, nc.tensor, nc.sync
    from concourse.bass import IndirectOffsetOnAxis

    NP = 2 * BATCH * S
    NT = NP // 128
    ag_d = nc.dram_tensor("ag_d", [NP, 8], F32, kind="ExternalInput").ap()
    xf8_d = nc.dram_tensor("xf8_d", [D, 8], F32, kind="ExternalInput").ap()
    ow_d = nc.dram_tensor("ow_d", [1, 1], F32, kind="ExternalInput").ap()
    xin_d = nc.dram_tensor("xin_d", [D, TOK], F32, kind="ExternalInput").ap()
    mw1_d = nc.dram_tensor("mw1_d", [8, D, 512], F32, kind="ExternalInput").ap()
    mw2_d = nc.dram_tensor("mw2_d", [8, 512, D], F32, kind="ExternalInput").ap()
    mw3_d = nc.dram_tensor("mw3_d", [8, D, 512], F32, kind="ExternalInput").ap()
    u8_d = nc.dram_tensor("u8_d", [8, 8], F32, kind="ExternalInput").ap()
    xo3_T = nc.dram_tensor("xo3_T", [D, TOK], F32, kind="ExternalOutput").ap()
    atab_d = nc.dram_tensor("atab_d", [64, 512], F32, kind="Internal").ap()
    btab_d = nc.dram_tensor("btab_d", [64, 512], F32, kind="Internal").ap()

    ident_f = nc.alloc_sbuf_tensor("ident_f", [128, 128], F32)
    ones1_f = nc.alloc_sbuf_tensor("ones1_f", [1, 128], F32)
    onescol = nc.alloc_sbuf_tensor("onescol", [128, 1], F32)
    u8 = nc.alloc_sbuf_tensor("u8", [8, 8], F32)
    iota8 = nc.alloc_sbuf_tensor("iota8", [128, 8], F32)
    iota64 = nc.alloc_sbuf_tensor("iota64", [128, 64], F32)
    riota = nc.alloc_sbuf_tensor("riota", [128, NT], F32)
    v.memset(ident_f[:], 1.0)
    v.memset(ones1_f[:], 1.0)
    p.inc("dve", v.memset(onescol[:], 1.0))
    p.wait(g, "dve")
    p.inc("pool", g.affine_select(ident_f[:], ident_f[:], [[-1, 128]],
                                  ALU.is_equal, 0.0, base=0, channel_multiplier=1))
    p.inc16("hw", sy.dma_start(out=u8[:], in_=u8_d))
    p.inc("pool", g.iota(iota8[:], [[1, 8]], base=0, channel_multiplier=0,
                         allow_small_or_imprecise_dtypes=True))
    p.inc("pool", g.iota(iota64[:], [[1, 64]], base=0, channel_multiplier=0,
                         allow_small_or_imprecise_dtypes=True))
    p.inc("pool", g.iota(riota[:], [[128, NT]], base=0, channel_multiplier=1,
                         allow_small_or_imprecise_dtypes=True))

    ps = [nc.alloc_psum_tensor(f"ps{i}", [128, 512], F32) for i in range(6)]
    ps.append(ps[0])
    ps_trb = nc.alloc_psum_tensor("ps_trb", [128, 64], BF16)

    xT = nc.alloc_sbuf_tensor("xT", [128, DT * TOK], F32)
    ag = nc.alloc_sbuf_tensor("ag", [128, NT * 8], F32)
    ind = nc.alloc_sbuf_tensor("ind", [128, NT * 8], F32)
    xf8b = nc.alloc_sbuf_tensor("xf8b", [128, DT * 8], BF16)
    w1e = nc.alloc_sbuf_tensor("w1e", [128, DT * 512], BF16)
    w3e = nc.alloc_sbuf_tensor("w3e", [128, DT * 512], BF16)
    atab = nc.alloc_sbuf_tensor("atab", [8, 8 * 512], F32)
    btab = nc.alloc_sbuf_tensor("btab", [8, 8 * 512], F32)
    btab64 = nc.alloc_sbuf_tensor("btab64", [64, 512], F32)
    ctsT = nc.alloc_sbuf_tensor("ctsT", [8, 1], F32)
    offsr = nc.alloc_sbuf_tensor("offsr", [1, 8], F32)
    offsb = nc.alloc_sbuf_tensor("offsb", [128, 8], F32)
    tmp8 = nc.alloc_sbuf_tensor("tmp8", [128, 8], F32)
    ecol = nc.alloc_sbuf_tensor("ecol", [128, 1], F32)
    wcol = nc.alloc_sbuf_tensor("wcol", [128, 1], F32)
    jcol = nc.alloc_sbuf_tensor("jcol", [128, 1], F32)
    keyf = nc.alloc_sbuf_tensor("keyf", [128, 1], F32)
    keyi = [nc.alloc_sbuf_tensor("keyi0", [128, 1], mybir.dt.int32),
            nc.alloc_sbuf_tensor("keyi1", [128, 1], mybir.dt.int32)]
    asb = nc.alloc_sbuf_tensor("asb", [128, 2 * 512], F32)
    hsb = nc.alloc_sbuf_tensor("hsb", [128, 512], F32)
    hbf = nc.alloc_sbuf_tensor("hbf", [128, 2 * 512], BF16)
    s64 = nc.alloc_sbuf_tensor("s64", [128, 2 * 64], BF16)
    gsb = nc.alloc_sbuf_tensor("gsb", [64, 512], F32)
    gbf = nc.alloc_sbuf_tensor("gbf", [64, 512], BF16)
    gtb = nc.alloc_sbuf_tensor("gtb", [128, 4 * 64], BF16)
    w2e = nc.alloc_sbuf_tensor("w2e", [128, 2 * 4 * D], BF16)
    ct8 = nc.alloc_sbuf_tensor("ct8", [8, D], F32)
    ctT = nc.alloc_sbuf_tensor("ctT", [128, 8], F32)
    maskc = nc.alloc_sbuf_tensor("maskc", [128, 1], F32)
    ps_trf = nc.alloc_psum_tensor("ps_trf", [128, 128], F32)

    p.inc16("hw", sy.dma_start(out=ag[:].rearrange("p (t e) -> p t e", e=8),
                           in_=ag_d.rearrange("(t p) e -> p t e", p=128)))
    for dt in range(DT):
        p.inc16("hw", sy.dma_start(out=xT[:, dt * TOK:(dt + 1) * TOK],
                                   in_=xin_d[128 * dt:128 * dt + 128, :]))
        p.inc16("gd", g.dma_start(out=xf8b[:, dt * 8:dt * 8 + 8],
                                  in_=xf8_d[128 * dt:128 * dt + 128, :]))
    p.inc16("hw", sy.dma_start(out=maskc[0:1, 0:1], in_=ow_d))
    p.join([v, t, sc, g])
    p.inc("pe", t.matmul(ps[6][:, 0:1], ones1_f[:], maskc[0:1, 0:1], start=True, stop=True))
    p.wait(sc, "pe")
    p.inc("act", sc.copy(maskc[:], ps[6][:, 0:1]))
    p.inc("dve", v.tensor_scalar(ind[:], ag[:], 0.0, None, ALU.is_gt))
    p.wait(t, "dve")
    p.wait(t, "act")
    for ti in range(NT):
        p.inc("pe", t.matmul(ps[5][0:8, 0:1], ind[:, ti * 8:ti * 8 + 8], onescol[:],
                             start=(ti == 0), stop=(ti == NT - 1)))
    p.wait(sc, "pe")
    p.inc("act", sc.copy(ctsT[:], ps[5][0:8, 0:1]))
    p.wait(t, "act")
    p.inc("pe", t.matmul(ps[5][0:1, 0:8], ctsT[:], u8[:], start=True, stop=True))
    p.wait(sc, "pe")
    p.inc("act", sc.copy(offsr[:], ps[5][0:1, 0:8]))
    p.wait(t, "act")
    p.inc("pe", t.matmul(ps[5][:, 0:8], ones1_f[:], offsr[:], start=True, stop=True))
    p.wait(sc, "pe")
    p.inc("act", sc.copy(offsb[:], ps[5][:, 0:8]))

    p.join([t, sc, g])
    for e in range(8):
        gsem = "gdw0"
        if e:
            p.wait(g, "pe")
        for dt in range(DT):
            p.inc16(gsem, g.dma_start(out=w1e[:, dt * 512:(dt + 1) * 512],
                                      in_=mw1_d[e, 128 * dt:128 * dt + 128, :]))
            p.inc16(gsem, g.dma_start(out=w3e[:, dt * 512:(dt + 1) * 512],
                                      in_=mw3_d[e, 128 * dt:128 * dt + 128, :]))
        p.wait(t, gsem)
        if e:
            p.wait(t, "act")
        for dt in range(DT):
            p.inc("pe", t.matmul(ps[0][0:8, :], xf8b[:, dt * 8:dt * 8 + 8],
                                 w1e[:, dt * 512:(dt + 1) * 512],
                                 start=(dt == 0), stop=(dt == DT - 1)))
            p.inc("pe", t.matmul(ps[1][0:8, :], xf8b[:, dt * 8:dt * 8 + 8],
                                 w3e[:, dt * 512:(dt + 1) * 512],
                                 start=(dt == 0), stop=(dt == DT - 1)))
        p.wait(sc, "pe")
        p.inc("act", sc.copy(atab[:, e * 512:(e + 1) * 512], ps[0][0:8, :]))
        p.inc("act", sc.copy(btab[:, e * 512:(e + 1) * 512], ps[1][0:8, :]))
    p.join([sy])
    p.inc16("hw", sy.dma_start(out=atab_d.rearrange("(e j) c -> j e c", e=8),
                           in_=atab[:].rearrange("j (e c) -> j e c", e=8)))
    p.inc16("hw", sy.dma_start(out=btab_d.rearrange("(e j) c -> j e c", e=8),
                           in_=btab[:].rearrange("j (e c) -> j e c", e=8)))
    p.wait(sy, "hw")
    p.inc16("hw", sy.dma_start(out=btab64[:], in_=btab_d))
    p.wait(sy, "hw")

    p.join([v, g, t, sc])
    gather_done = [0, 0]
    mm_done = [0, 0]
    for ti in range(NT):
        r = ti % 2
        p.wait(v, "pe", mm_done[r])
        p.inc("dve", v.tensor_scalar(tmp8[:], offsb[:], riota[:, ti:ti + 1], None, ALU.is_gt))
        p.wait(v, "dve")
        p.inc("dve", v.reduce_sum(ecol[:], tmp8[:], axis=AX.X, negate=True))
        p.inc("dve", v.reduce_sum(wcol[:], ag[:, ti * 8:ti * 8 + 8], axis=AX.X))
        p.wait(v, "dve")
        p.inc("dve", v.tensor_mul(tmp8[:], ind[:, ti * 8:ti * 8 + 8], iota8[:]))
        p.wait(v, "dve")
        p.inc("dve", v.reduce_sum(jcol[:], tmp8[:], axis=AX.X))
        p.wait(v, "dve")
        p.inc("dve", v.scalar_tensor_tensor(keyf[:], ecol[:], 8.0, jcol[:],
                                            ALU.mult, ALU.add))
        p.wait(v, "dve")
        p.inc("dve", v.tensor_scalar(keyf[:], keyf[:], 64.0, None, ALU.add))
        p.wait(v, "dve")
        p.inc("dve", v.tensor_copy(keyi[r][:], keyf[:]))
        p.wait(v, "dve")
        p.inc("dve", v.tensor_scalar(s64[:, r * 64:(r + 1) * 64], iota64[:], keyf[:],
                                     None, ALU.is_equal))
        p.wait(g, "dve")
        if ti >= 2:
            p.wait(g, "act")
        p.inc16("gd%d" % r, g.indirect_dma_start(
            out=asb[:, r * 512:(r + 1) * 512], out_offset=None, in_=atab_d,
            in_offset=IndirectOffsetOnAxis(keyi[r][:], 0)))
        p.wait(sc, "gd%d" % r)
        p.wait(sc, "dve")
        if ti >= 2:
            p.wait(sc, "pe", mm_done[r])
        p.inc("act", sc.activation(hsb[:], asb[:, r * 512:(r + 1) * 512], AF.Silu,
                                   scale=wcol[:]))
        p.wait(v, "act")
        p.inc("dve", v.tensor_scalar(hbf[:, r * 512:(r + 1) * 512], hsb[:], wcol[:],
                                     None, ALU.mult))
        p.wait(t, "dve")
        mm_done[r] = p.inc("pe", t.matmul(ps[3][0:64, :], s64[:, r * 64:(r + 1) * 64],
                                          hbf[:, r * 512:(r + 1) * 512],
                                          start=(ti == 0), stop=(ti == NT - 1)))
    p.wait(sc, "pe")
    p.inc("act", sc.copy(gsb[:], ps[3][0:64, :]))
    p.wait(v, "act")
    p.inc("dve", v.tensor_mul(gsb[:], gsb[:], btab64[:]))
    p.wait(v, "dve")
    p.inc("dve", v.tensor_copy(gbf[:], gsb[:]))
    p.wait(t, "dve")
    for ht in range(4):
        if ht:
            p.wait(t, "act")
        p.inc("pe", t.transpose(ps_trb[:], gbf[0:64, ht * 128:(ht + 1) * 128],
                                ident_bf3(nc, p, v, g)[0:64, 0:64]))
        p.wait(sc, "pe")
        p.inc("act", sc.copy(gtb[:, ht * 64:(ht + 1) * 64], ps_trb[:]))
    p.join([g, t])
    for e in range(8):
        gsem = "gdw%d" % (e % 2)
        if e >= 2:
            p.wait(g, "pe")
        for ht in range(4):
            p.inc16(gsem, g.dma_start(out=w2e[:, (e % 2) * 4 * D + ht * D:(e % 2) * 4 * D + (ht + 1) * D],
                                      in_=mw2_d[e, 128 * ht:128 * ht + 128, :]))
        p.wait(t, gsem)
        for half in range(2):
            for ht in range(4):
                p.inc("pe", t.matmul(
                    ps[4 + half][0:8, :],
                    gtb[:, ht * 64 + 8 * e: ht * 64 + 8 * e + 8],
                    w2e[:, (e % 2) * 4 * D + ht * D + half * 512:
                        (e % 2) * 4 * D + ht * D + half * 512 + 512],
                    start=(e == 0 and ht == 0), stop=(e == 7 and ht == 3)))
    p.wait(sc, "pe")
    p.inc("act", sc.copy(ct8[:, 0:512], ps[4][0:8, :]))
    p.inc("act", sc.copy(ct8[:, 512:1024], ps[5][0:8, :]))
    for dt in range(DT):
        p.wait(t, "act")
        p.inc("pe", t.transpose(ps_trf[:, 0:8], ct8[0:8, 128 * dt:128 * dt + 128],
                                ident_f[0:8, 0:8]))
        p.wait(sc, "pe")
        p.wait(sc, "dve")
        p.inc("act", sc.copy(ctT[:], ps_trf[:, 0:8]))
        p.wait(v, "act")
        p.inc("dve", v.tensor_scalar(ctT[:], ctT[:], maskc[:], None, ALU.mult))
        p.wait(v, "dve")
        p.inc("dve", v.tensor_add(xT[:, dt * TOK:dt * TOK + 8],
                                  xT[:, dt * TOK:dt * TOK + 8], ctT[:]))
    p.join([sy])
    for dt in range(DT):
        p.inc16("hw", sy.dma_start(out=xo3_T[128 * dt:128 * dt + 128, :],
                                   in_=xT[:, dt * TOK:(dt + 1) * TOK]))
    p.wait(sy, "hw")
    return nc


_IDENT_BF3 = {}


def ident_bf3(nc, p, v, g):
    if id(nc) not in _IDENT_BF3:
        ib = nc.alloc_sbuf_tensor("ident_bf3", [128, 128], BF16)
        p.inc("dve", v.memset(ib[:], 1.0))
        p.wait(g, "dve")
        p.inc("pool", g.affine_select(ib[:], ib[:], [[-1, 128]], ALU.is_equal, 0.0,
                                      base=0, channel_multiplier=1))
        p.wait(nc.tensor, "pool")
        _IDENT_BF3[id(nc)] = ib
    return _IDENT_BF3[id(nc)]


def _swizzle_full(xb, h):
    blocks = xb.reshape(16, 16, 16, D)
    sw = np.concatenate([blocks[:, h::2], blocks[:, 1 - h::2]], axis=1)
    return np.ascontiguousarray(sw.reshape(S, D).T)


def _own_rows_idx(h):
    g = 2 * np.arange(8) + h
    return (g[:, None] * 256 + np.arange(256)[None, :]).reshape(-1)


def kernel(**inputs):
    x = np.asarray(inputs["L0_attn_x"])
    Wqkv = np.asarray(inputs["L0_attn_W_qkv"])
    bqkv = np.asarray(inputs["L0_attn_b_qkv"])
    Wo = np.asarray(inputs["L0_attn_W_o"])
    bo = np.asarray(inputs["L0_attn_b_o"])
    W1, b1 = np.asarray(inputs["L0_ffn_W1"]), np.asarray(inputs["L0_ffn_b1"])
    W2, b2 = np.asarray(inputs["L0_ffn_W2"]), np.asarray(inputs["L0_ffn_b2"])
    Wr, br = np.asarray(inputs["L0_ffn_Wr"]), np.asarray(inputs["L0_ffn_br"])
    Wdq, Wdkv = np.asarray(inputs["L1_attn_W_dq"]), np.asarray(inputs["L1_attn_W_dkv"])
    Wuq, Wuk, Wuv = (np.asarray(inputs["L1_attn_W_uq"]),
                     np.asarray(inputs["L1_attn_W_uk"]),
                     np.asarray(inputs["L1_attn_W_uv"]))
    Wo1 = np.asarray(inputs["L1_attn_W_o"])
    wg = np.asarray(inputs["L1_ffn_w_gate"])
    eb = np.asarray(inputs["L1_ffn_expert_bias"])
    mw1, mw2, mw3 = (np.asarray(inputs["L1_ffn_w1"]), np.asarray(inputs["L1_ffn_w2"]),
                     np.asarray(inputs["L1_ffn_w3"]))
    sw1, sw2, sw3 = (np.asarray(inputs["L1_ffn_sw1"]), np.asarray(inputs["L1_ffn_sw2"]),
                     np.asarray(inputs["L1_ffn_sw3"]))

    nc1 = build_launch1()
    in1 = []
    for c in range(NCORE):
        b, h = c // 2, c % 2
        xb = x[b]
        in1.append({
            "x_fullT": _swizzle_full(xb, h),
            "x_ownT": np.ascontiguousarray(xb[_own_rows_idx(h)].T),
            "w_qkv": Wqkv, "b_qkv": bqkv[None, :],
            "w_o0": Wo, "b_o0": bo[None, :],
            "w1_d": W1, "b1_d": b1, "w2_d": W2, "b2_d": b2,
            "wr_d": Wr, "br_d": br[None, :],
            "wdq_d": Wdq, "wdkv_d": Wdkv,
        })
    r1 = run_bass_kernel_spmd(nc1, in1, core_ids=list(range(NCORE)))

    nc2 = build_launch2()
    kp = np.concatenate([_own_rows_idx(0), _own_rows_idx(1)]).astype(np.float32)
    kposT = np.ascontiguousarray(kp.reshape(32, 128).T)
    in2 = []
    for c in range(NCORE):
        b, h = c // 2, c % 2
        ckv_full = np.concatenate([np.asarray(r1.results[2 * b]["ckv_o"]),
                                   np.asarray(r1.results[2 * b + 1]["ckv_o"])], axis=1)
        in2.append({
            "x_inT": np.asarray(r1.results[c]["xo_T"]),
            "cq_i": np.asarray(r1.results[c]["cq_o"]),
            "ckv_i": ckv_full,
            "wuq_d": Wuq, "wuk_d": Wuk, "wuv_d": Wuv, "wo1_d": Wo1,
            "qpos_d": _own_rows_idx(h).astype(np.float32)[None, :],
            "kpos_d": kposT,
            "wg_d": wg, "eb_d": eb[None, :],
            "mw1_d": mw1, "mw2_d": mw2, "mw3_d": mw3,
            "sw1_d": sw1, "sw2_d": sw2, "sw3_d": sw3,
        })
    r2 = run_bass_kernel_spmd(nc2, in2, core_ids=list(range(NCORE)))

    A_g = np.empty((2 * BATCH * S, 8), np.float32)
    for c in range(NCORE):
        b, h = c // 2, c % 2
        A0 = np.asarray(r2.results[c]["a0_o"]).reshape(128, 16, 8).transpose(1, 0, 2).reshape(TOK, 8)
        A1 = np.asarray(r2.results[c]["a1_o"]).reshape(128, 16, 8).transpose(1, 0, 2).reshape(TOK, 8)
        t_glob = b * S + _own_rows_idx(h)
        A_g[2 * t_glob] = A0
        A_g[2 * t_glob + 1] = A1

    nc3 = build_launch3()
    in3 = []
    for c in range(NCORE):
        in3.append({
            "ag_d": A_g,
            "xf8_d": np.asarray(r2.results[c]["xf8_o"]),
            "ow_d": np.array([[1.0 if c == 0 else 0.0]], np.float32),
            "xin_d": np.asarray(r2.results[c]["xo2_T"]),
            "mw1_d": mw1, "mw2_d": mw2, "mw3_d": mw3,
            "u8_d": np.triu(np.ones((8, 8), np.float32)),
        })
    r3 = run_bass_kernel_spmd(nc3, in3, core_ids=list(range(NCORE)))

    out = np.empty((BATCH, S, D), np.float32)
    for c in range(NCORE):
        b, h = c // 2, c % 2
        out[b, _own_rows_idx(h), :] = np.asarray(r3.results[c]["xo3_T"]).T
    return out
